# version 01 (final)
# baseline (speedup 1.0000x reference)
import numpy as np
import ml_dtypes
import concourse.bass as bass
import concourse.mybir as mybir
from concourse.bass_utils import run_bass_kernel_spmd
from concourse.alu_op_type import AluOpType as ALU

F32 = mybir.dt.float32
BF16 = mybir.dt.bfloat16
AF = mybir.ActivationFunctionType

B = 2
S = 2048
D = 1024
EQK = 128
KSEL = 8192
NOB = 16
HALF = 1024
EPS = 1e-5

MU = float(np.sqrt(0.5))
SIGC = float(np.sqrt(0.25 * np.pi))
C1 = 1.0 / (S * SIGC * np.sqrt(2.0))
CREL = float(np.sqrt(128.0) / (SIGC * np.sqrt(2.0)))
C0 = float(-MU / (SIGC * np.sqrt(2.0)))


def tl(x):
    if x is None:
        return []
    if isinstance(x, list):
        return [t for t in x if t is not None]
    return [x]


class Sched:
    def __init__(self, nc, n_dsem=18):
        self.nc = nc
        self.ops = {e: [] for e in ("T", "V", "S", "Y", "G")}
        self.psem = {e: nc.alloc_semaphore(f"p{e}") for e in ("T", "V", "S", "G")}
        self.pcnt = {e: 0 for e in ("T", "V", "S", "G")}
        self.dsems = [nc.alloc_semaphore(f"dm{i}") for i in range(n_dsem)]
        self.dtot = [0] * n_dsem
        self.didx = 0
        self.gsems = [nc.alloc_semaphore(f"gm{i}") for i in range(4)]
        self.gtot = [0] * 4
        self.gidx = 0

    def op(self, e, emit, waits=(), sig=False, fwaits=()):
        tok = None
        inc = None
        if sig:
            self.pcnt[e] += 1
            tok = (self.psem[e], self.pcnt[e])
            inc = (self.psem[e], 1)
        ws = []
        for w in waits:
            ws += tl(w)
        fs = []
        for w in fwaits:
            fs += tl(w)
        self.ops[e].append((emit, ws + [("F", t) for t in fs], inc))
        return tok

    def dma(self, qe, out, in_, waits=(), **kw):
        if qe == "G":
            i = self.gidx
            self.gidx = (self.gidx + 1) % len(self.gsems)
            sem = self.gsems[i]
            pre = (sem, self.gtot[i])
            self.gtot[i] += 16
            tok = (sem, self.gtot[i])
        else:
            i = self.didx
            self.didx = (self.didx + 1) % len(self.dsems)
            sem = self.dsems[i]
            pre = (sem, self.dtot[i])
            self.dtot[i] += 16
            tok = (sem, self.dtot[i])

        def emit(eng, out=out, in_=in_, sem=sem, kw=kw):
            return eng.dma_start(out=out, in_=in_, **kw).then_inc(sem, 16)

        ws = []
        for w in waits:
            ws += tl(w)
        if pre[1] > 0:
            ws.append(pre)
        self.ops[qe].append((emit, ws, None))
        return tok

    def emit_all(self):
        nc = self.nc
        sched = self

        def mk_runner(e):
            def run(eng):
                seen = {}
                own = sched.psem.get(e)
                pos = 0
                for emit, waits, inc in sched.ops[e]:
                    if inc is not None:
                        pos += 1
                    for w in waits:
                        if isinstance(w[0], str):
                            sem, val = w[1]
                            eng.wait_ge(sem, val)
                            key = sem.name if hasattr(sem, "name") else id(sem)
                            seen[key] = max(seen.get(key, -1), val)
                            continue
                        sem, val = w
                        if own is not None and sem is own:
                            continue
                        key = sem.name if hasattr(sem, "name") else id(sem)
                        if seen.get(key, -1) >= val:
                            continue
                        eng.wait_ge(sem, val)
                        seen[key] = val
                    ins = emit(eng)
                    if inc is not None:
                        ins.then_inc(inc[0], inc[1])
            return run

        with nc.Block() as block:
            block.tensor(mk_runner("T"))
            block.vector(mk_runner("V"))
            block.scalar(mk_runner("S"))
            block.sync(mk_runner("Y"))
            block.gpsimd(mk_runner("G"))


class MK:
    def __init__(self, upto=4, taps=(), ab_batches=2, ab_tblks=8, ab_phase=4):
        self.upto = upto
        self.ab_batches = ab_batches
        self.ab_tblks = ab_tblks
        self.ab_phase = ab_phase
        self.taps = set(taps)
        nc = bass.Bass("TRN2", target_bir_lowering=False,
               dynamic_dma_scratch_size=12288)
        self.nc = nc
        self.sc = Sched(nc)
        self.out_toks = []
        self.ln_w_toks = []
        self.fr = {}
        self._declare_io()
        self._alloc_sbuf()
        self.ps = [nc.alloc_psum_tensor(f"ps{i}", [128, 512], F32) for i in range(8)]
        self.ps_free = [None] * 8

    def getf(self, name):
        return self.fr.get(name)

    def setf(self, name, tok):
        self.fr[name] = tok

    def fence(self, names, tok):
        for n in names:
            self.fr[n] = tok

    def _declare_io(self):
        nc = self.nc
        dt = nc.dram_tensor
        self.d_x = dt("x", [B * S, D], F32, kind="ExternalInput")
        self.d_y = dt("y", [B * S, D], F32, kind="ExternalOutput")
        self.d_wqk = dt("wqk", [D, EQK], F32, kind="ExternalInput")
        self.d_bqk = dt("bqk", [EQK, 1], F32, kind="ExternalInput")
        self.d_g0 = dt("g0", [EQK, 1], F32, kind="ExternalInput")
        self.d_b0 = dt("b0", [EQK, 1], F32, kind="ExternalInput")
        self.d_g1 = dt("g1", [EQK, 1], F32, kind="ExternalInput")
        self.d_b1 = dt("b1", [EQK, 1], F32, kind="ExternalInput")
        self.d_wv = dt("wv", [D, D], F32, kind="ExternalInput")
        self.d_bv = dt("bv", [1, D], F32, kind="ExternalInput")
        self.d_rel = dt("rel", [32, 1], F32, kind="ExternalInput")
        self.d_oht = dt("oht", [32, S], F32, kind="ExternalInput")
        self.d_tmb = dt("tmb", [128, 896], F32, kind="ExternalInput")
        self.d_selw = dt("selw", [D, KSEL], F32, kind="ExternalInput")
        self.d_selbB = dt("selbB", [128, KSEL], F32, kind="ExternalInput")
        self.d_compw = dt("compw", [NOB, D, 512], F32, kind="ExternalInput")
        self.d_compb = dt("compb", [1, KSEL], F32, kind="ExternalInput")
        self.d_win = dt("win", [D, D], F32, kind="ExternalInput")
        self.d_bin = dt("bin", [1, D], F32, kind="ExternalInput")
        self.d_wgx = dt("wgx", [D, D], F32, kind="ExternalInput")
        self.d_bgx = dt("bgx", [128, 8], F32, kind="ExternalInput")
        self.d_wga = dt("wga", [D, D], F32, kind="ExternalInput")
        self.d_bga = dt("bga", [128, 8], F32, kind="ExternalInput")
        self.d_apar = dt("apar", [128, 8], F32, kind="ExternalInput")
        self.d_wout = dt("wout", [D, D], F32, kind="ExternalInput")
        self.d_bout = dt("bout", [1, D], F32, kind="ExternalInput")
        self.d_xs0 = dt("xs0", [B * S, D], F32, kind="Internal")
        self.d_xs1 = dt("xs1", [B * S, D], F32, kind="Internal")
        self.d_lnscr = dt("lnscr", [B * S, D], BF16, kind="Internal")
        self.d_erow = dt("erow", [1, 2688], F32, kind="Internal")
        self.d_selw_bf = dt("selw_bf", [NOB, D, 512], BF16, kind="Internal")
        self.d_compw_bf = dt("compw_bf", [NOB, D, 512], BF16, kind="Internal")
        self.taps_d = {}
        for t in self.taps:
            shape = {"ln1T": [128, 8 * S], "qkT": [128, S], "qT": [128, S],
                     "kT": [128, S], "v": [128, 16 * D], "attnT": [128, 16 * 512],
                     "megaT": [128, 2432], "xa": [B * S, D], "ln2T": [128, 8 * S],
                     "xf": [B * S, D], "xb": [B * S, D], "xinT": [128, 8 * S],
                     "h": [128, 16 * 1024], "outc": [128, 8192],
                     "ffdbg": [128, 512], "ffdbgm": [128, 1024],
                     "slotc": [128, 4096], "psumc": [128, 512],
                     "gdbg": [128, 4 * 2048], "gdbg2": [128, 2 * 2048],
                     }[t]
            dtp = BF16 if t in ("ln1T", "qT", "kT", "v", "attnT", "megaT", "ln2T",
                                "xinT", "h", "ffdbgm", "slotc") else F32
            self.taps_d[t] = self.nc.dram_tensor("tap_" + t, shape, dtp,
                                                 kind="ExternalOutput")

    def _alloc_sbuf(self):
        A = self.nc.alloc_sbuf_tensor
        self.ones_row = A("ones_row", [1, 512], BF16)
        self.ones_col = self.ones_row[0:1, 0:128]
        self.onec = A("onec", [128, 1], F32)
        self.bqk = A("bqk_c", [128, 1], F32)
        self.g0c = A("g0c", [128, 1], F32)
        self.b0c = A("b0c", [128, 1], F32)
        self.g1c = A("g1c", [128, 1], F32)
        self.b1c = A("b1c", [128, 1], F32)
        self.bv_bf = A("bv_bf", [1, D], BF16)
        self.bin_bf = A("bin_bf", [1, D], BF16)
        self.bout_bf = A("bout_bf", [1, D], BF16)
        self.compb_s = [A("compb0", [1, 512], BF16), A("compb1", [1, 512], BF16)]
        self.bgx = A("bgx_sb", [128, 8], F32)
        self.bga = A("bga_sb", [128, 8], F32)
        self.sa8 = A("sa8", [128, 8], F32)
        self.rel32 = A("rel32", [32, 1], F32)
        self.stat = A("stat_sb", [128, 64], F32)
        self.carry = A("carry_sb", [128, 8], F32)
        self.lnT = [A(f"lnT{i}", [128, 8 * S], BF16, is_transpose_dest=True)
                    for i in range(2)]
        self.wa = [A(f"wa{i}", [128, 8 * D], BF16) for i in range(3)]
        self.big = A("big", [128, 24 * 1024], BF16)
        self.scr = A("scr", [128, 12 * 1024], BF16)
        self.xold = [A(f"xold{i}", [128, D], F32) for i in range(3)]
        self.lnb = [A(f"lnb{i}", [128, D], BF16) for i in range(2)]
        self.megaT = self.scr[:, 4096:8960].bitcast(F32)
        self.tmb = self.scr[:, 8960:9856]
        self.e_sb = self.big[:, 18432:23808].bitcast(F32)[0:1, :]

    def big_f32(self, off_el, n_el):
        return self.big[:, 2 * off_el:2 * (off_el + n_el)].bitcast(F32)

    def scr_f32(self, off_el, n_el):
        return self.scr[:, 2 * off_el:2 * (off_el + n_el)].bitcast(F32)

    def p0(self):
        sc = self.sc
        t_or = sc.op("G", lambda e: e.memset(self.ones_row[:, :], 1.0), sig=True)
        t_ones = t_or
        t_oc = sc.op("G", lambda e: e.memset(self.onec[:, :], 1.0), sig=True)
        t_cz = sc.op("G", lambda e: e.memset(self.carry[:, :], 0.0), sig=True)
        t_esb = sc.op("G", lambda e: e.memset(self.e_sb[:, :], C0), sig=True)
        self.t_ones = [t_or]
        self.t_onec = t_oc
        self.setf("carry", t_cz)

        tk = {}
        tk["bqk"] = sc.dma("Y", self.bqk[:, :], self.d_bqk.ap())
        tk["g0"] = sc.dma("Y", self.g0c[:, :], self.d_g0.ap())
        tk["b0"] = sc.dma("Y", self.b0c[:, :], self.d_b0.ap())
        tk["g1"] = sc.dma("Y", self.g1c[:, :], self.d_g1.ap())
        tk["b1"] = sc.dma("Y", self.b1c[:, :], self.d_b1.ap())
        tk["tmb"] = sc.dma("G", self.tmb[:, :], self.d_tmb.ap())
        tk["bgx"] = sc.dma("Y", self.bgx[:, :], self.d_bgx.ap())
        tk["bga"] = sc.dma("Y", self.bga[:, :], self.d_bga.ap())
        tk["rel"] = sc.dma("Y", self.rel32[:, :], self.d_rel.ap())
        self.t_tmb = tk["tmb"]
        self.t_bv = sc.dma("G", self.bv_bf[:, :], self.d_bv.ap())
        self.t_bin = sc.dma("G", self.bin_bf[:, :], self.d_bin.ap())
        self.t_bout = sc.dma("G", self.bout_bf[:, :], self.d_bout.ap())

        t_g0 = sc.op("V", lambda e: e.tensor_scalar_mul(self.g0c[:, :],
                                                        self.g0c[:, :], C1),
                     waits=[tk["g0"]], sig=True)
        t_b0 = sc.op("V", lambda e: e.tensor_scalar_mul(self.b0c[:, :],
                                                        self.b0c[:, :], C1),
                     waits=[tk["b0"]], sig=True)
        self.t_qparams = [t_g0, t_b0, tk["g1"], tk["b1"], tk["bqk"]]

        tk["apar"] = sc.dma("Y", self.sa8[:, :], self.d_apar.ap())
        ab = self.scr_f32(0, 8)
        ex = self.scr_f32(8, 8)
        mx = self.scr_f32(16, 8)
        t1_ = sc.op("S", lambda e: e.activation(ab, self.sa8[:, :], AF.Abs),
                    waits=[tk["apar"]], sig=True)
        t2_ = sc.op("S", lambda e: e.activation(ex, ab, AF.Exp, scale=-1.0),
                    fwaits=[t1_], sig=True)
        t3 = sc.op("S", lambda e: e.activation(ex, ex, AF.Ln, bias=self.onec[:, :]),
                   waits=[t_oc], fwaits=[t2_], sig=True)
        t4_ = sc.op("V", lambda e: e.tensor_scalar_max(mx, self.sa8[:, :], 0.0),
                    waits=[tk["apar"]], sig=True)
        t5_ = sc.op("V", lambda e: e.tensor_add(mx, mx, ex), waits=[t3],
                    fwaits=[t4_], sig=True)
        t6 = sc.op("V", lambda e: e.tensor_scalar_mul(self.sa8[:, :], mx, -8.0),
                   fwaits=[t5_], sig=True)
        self.t_sa8 = t6

        ohtap = self.lnT[1][:, 0:4096].bitcast(F32)[0:32, :]
        tk["oht"] = sc.dma("Y", ohtap, self.d_oht.ap())
        t7 = sc.op("S", lambda e: e.activation(self.rel32[:, :], self.rel32[:, :],
                                               AF.Copy, scale=CREL),
                   waits=[tk["rel"]], sig=True)
        ev = []
        for blk in range(4):
            pb = self.ps[blk]
            tmm = sc.op("T", lambda e, blk=blk, pb=pb: e.matmul(
                pb[0:1, :], self.rel32[:, :], ohtap[:, blk * 512:(blk + 1) * 512],
                start=True, stop=True),
                waits=[t7, tk["oht"], self.ps_free[blk]], sig=True)
            tev = sc.op("V", lambda e, blk=blk, pb=pb: e.tensor_scalar_add(
                self.e_sb[0:1, 640 + blk * 512:640 + (blk + 1) * 512],
                pb[0:1, :], C0), waits=[tmm, t_esb], sig=True)
            self.ps_free[blk] = tev
            ev.append(tev)
        t_erow = sc.dma("Y", self.d_erow.ap(), self.e_sb[:, :], waits=ev)
        mts = []
        for p in range(128):
            tkp = sc.dma("Y", self.megaT[p:p + 1, :],
                         self.d_erow.ap()[0:1, 256 - p:256 - p + 2432],
                         waits=[t_erow])
            if p % 32 == 31:
                mts.append(tkp)
        self.t_megaT = mts
        self.t_rel_done = ev

        st_q = self.big_f32(0, 1024)
        st_v = self.big_f32(1024, 8192)
        tq = sc.dma("Y", st_q, self.d_wqk.ap().rearrange("(dc p) e -> p dc e",
                                                         p=128))
        t8 = sc.op("S", lambda e: e.copy(self.wa[0][:, 0:1024], st_q),
                   waits=[tq], sig=True)
        tv = sc.dma("Y", st_v, self.d_wv.ap().rearrange("(dc p) e -> p dc e",
                                                        p=128))
        t9 = sc.op("S", lambda e: e.copy(self.wa[1][:, :], st_v), waits=[tv],
                   sig=True)
        self.t_wqk = t8
        self.t_wv = t9
        self.setf("big", [t8, t9])

        if "megaT" in self.taps:
            self.out_toks.append(sc.dma("Y", self.taps_d["megaT"].ap(),
                                        self.megaT[:, :], waits=mts))

    def ln_stats(self, xo_ap, ii, toks):
        sc = self.sc
        st = self.stat
        c = 32 * ii
        bn6a = st[:, c:c + 6]
        bn6b = st[:, c + 6:c + 12]
        sm_a = st[:, c + 12:c + 13]
        sm_b = st[:, c + 13:c + 14]
        cv_a = st[:, c + 14:c + 15]
        cv_b = st[:, c + 15:c + 16]
        sm = st[:, c + 16:c + 17]
        sq1 = st[:, c + 17:c + 18]
        cvs = st[:, c + 18:c + 19]
        mean = st[:, c + 19:c + 20]
        var = st[:, c + 20:c + 21]
        rstd = st[:, c + 21:c + 22]
        msq = st[:, c + 22:c + 26]
        q01 = st[:, c + 26:c + 27]
        q23 = st[:, c + 27:c + 28]
        nm2 = st[:, c + 28:c + 29]
        sc.op("V", lambda e: e.bn_stats(bn6a, xo_ap[:, 0:512]), waits=toks)
        t_bn = sc.op("V", lambda e: e.bn_stats(bn6b, xo_ap[:, 512:1024]),
                     sig=True)
        sc.op("V", lambda e: e.tensor_mul(msq[:, 0:1], bn6a[:, 1:2], bn6a[:, 1:2]),
              fwaits=[t_bn])
        sc.op("V", lambda e: e.tensor_mul(msq[:, 1:2], bn6a[:, 4:5], bn6a[:, 4:5]))
        sc.op("V", lambda e: e.tensor_mul(msq[:, 2:3], bn6b[:, 1:2], bn6b[:, 1:2]))
        sc.op("V", lambda e: e.tensor_mul(msq[:, 3:4], bn6b[:, 4:5], bn6b[:, 4:5]))
        sc.op("V", lambda e: e.tensor_add(sm_a, bn6a[:, 1:2], bn6a[:, 4:5]))
        sc.op("V", lambda e: e.tensor_add(sm_b, bn6b[:, 1:2], bn6b[:, 4:5]))
        sc.op("V", lambda e: e.tensor_add(cv_a, bn6a[:, 2:3], bn6a[:, 5:6]))
        t0 = sc.op("V", lambda e: e.tensor_add(cv_b, bn6b[:, 2:3], bn6b[:, 5:6]),
                   sig=True)
        sc.op("V", lambda e: e.tensor_add(sm, sm_a, sm_b), fwaits=[t0])
        sc.op("V", lambda e: e.tensor_add(q01, msq[:, 0:1], msq[:, 1:2]))
        sc.op("V", lambda e: e.tensor_add(q23, msq[:, 2:3], msq[:, 3:4]))
        t1 = sc.op("V", lambda e: e.tensor_add(cvs, cv_a, cv_b), sig=True)
        sc.op("V", lambda e: e.tensor_scalar_mul(mean, sm, 0.25), fwaits=[t1])
        sc.op("V", lambda e: e.tensor_add(sq1, q01, q23))
        t2_ = sc.op("V", lambda e: e.tensor_scalar_mul(var, cvs, 1.0 / D),
                    sig=True)
        sc.op("V", lambda e: e.scalar_tensor_tensor(var, sq1, 0.25, var,
                                                    ALU.mult, ALU.add),
              fwaits=[t2_])
        t3 = sc.op("V", lambda e: e.scalar_tensor_tensor(nm2, mean, -1.0, mean,
                                                         ALU.mult, ALU.mult),
                   sig=True)
        t4 = sc.op("V", lambda e: e.scalar_tensor_tensor(var, var, 1.0, nm2,
                                                         ALU.mult, ALU.add),
                   fwaits=[t3], sig=True)
        t5 = sc.op("V", lambda e: e.tensor_scalar_add(var, var, EPS),
                   fwaits=[t4], sig=True)
        t_r1 = sc.op("V", lambda e: e.reciprocal(rstd, var), fwaits=[t5],
                     sig=True)
        t_r2 = sc.op("S", lambda e: e.activation(rstd, rstd, AF.Sqrt),
                     waits=[t_r1], sig=True)
        return t_r2, mean, rstd

    def boundary2(self, dst_x, row0, xo_ap, toks, ii, last=False):
        sc = self.sc
        t_wx = sc.dma("Y", dst_x[row0:row0 + 128, :], xo_ap, waits=toks)
        self.setf(f"xw{row0}", t_wx)
        if last:
            self.out_toks.append(t_wx)
            return [t_wx]
        t_r2, mean, rstd = self.ln_stats(xo_ap, ii, toks)
        lnb = self.lnb[ii]
        t_n = sc.op("V", lambda e: e.tensor_scalar(lnb[:, :], xo_ap, mean, rstd,
                                                   ALU.subtract, ALU.mult),
                    waits=[t_r2, self.getf(f"lnbw{ii}")], sig=True)
        t_wl = sc.dma("Y", self.d_lnscr.ap()[row0:row0 + 128, :], lnb[:, :],
                      waits=[t_n])
        self.setf(f"lnbw{ii}", t_wl)
        self.ln_w_toks.append(t_wl)
        return [t_n, t_wx]

    def ln1(self, b):
        sc = self.sc
        xin = self.d_x.ap()
        for tt in range(16):
            row0 = b * S + tt * 128
            xi = tt % 3
            xo = self.xold[xi]
            t_ld = sc.dma("Y", xo[:, :], xin[row0:row0 + 128, :],
                          waits=[self.getf(f"xold{xi}")])
            ii = tt % 2
            lnb = self.lnb[ii]
            t_r2, mean, rstd = self.ln_stats(xo[:, :], ii, [t_ld])
            t_n = sc.op("V", lambda e, lnb=lnb, xo=xo, mean=mean, rstd=rstd:
                        e.tensor_scalar(lnb[:, :], xo[:, :], mean, rstd,
                                        ALU.subtract, ALU.mult),
                        waits=[t_r2, self.getf(f"lnbw{ii}")], sig=True)
            self.setf(f"xold{xi}", t_n)
            t_wl = sc.dma("Y", self.d_lnscr.ap()[row0:row0 + 128, :], lnb[:, :],
                          waits=[t_n])
            self.setf(f"lnbw{ii}", t_wl)
            self.ln_w_toks.append(t_wl)

    def transposes(self, b, slot, lnw_toks, extra=None):
        sc = self.sc
        toks = []
        for dc in range(8):
            t = sc.dma("S", self.lnT[slot][:, dc * S:(dc + 1) * S],
                       self.d_lnscr.ap()[b * S:(b + 1) * S,
                                         dc * 128:(dc + 1) * 128],
                       waits=lnw_toks + tl(extra) + tl(self.getf(f"lnT{slot}")),
                       transpose=True)
            toks.append(t)
        return toks

    def attn_a(self, b, lnT_toks):
        sc = self.sc
        lnT = self.lnT[b]
        qk = self.wa[2][:, 0:4096].bitcast(F32)
        qT = self.wa[2][:, 4096:6144]
        kT = self.wa[2][:, 6144:8192]
        t_qk = []
        for tb in range(4):
            pb = self.ps[tb]
            tm = None
            for dc in range(8):
                tm = sc.op("T", lambda e, pb=pb, dc=dc, tb=tb: e.matmul(
                    pb[:, :], self.wa[0][:, dc * 128:(dc + 1) * 128],
                    lnT[:, dc * S + tb * 512:dc * S + tb * 512 + 512],
                    start=(dc == 0), stop=(dc == 7)),
                    waits=([self.t_wqk, self.ps_free[tb]] + lnT_toks +
                           tl(self.getf("wa2"))) if dc == 0 else [],
                    sig=(dc == 7))
            t_ev = sc.op("S", lambda e, pb=pb, tb=tb: e.activation(
                qk[:, tb * 512:(tb + 1) * 512], pb[:, :], AF.Silu,
                bias=self.bqk[:, :]), waits=[tm] + self.t_qparams, sig=True)
            self.ps_free[tb] = t_ev
            t_qk.append(t_ev)
        t_q = sc.op("V", lambda e: e.tensor_scalar(qT[:, :], qk[:, :],
                                                   self.g0c[:, :], self.b0c[:, :],
                                                   ALU.mult, ALU.add),
                    waits=t_qk, sig=True)
        t_k = sc.op("V", lambda e: e.tensor_scalar(kT[:, :], qk[:, :],
                                                   self.g1c[:, :], self.b1c[:, :],
                                                   ALU.mult, ALU.add), sig=True)
        v_sb = self.big
        t_v = []
        for tt in range(16):
            evs = []
            for eb in range(2):
                pb = self.ps[4 + eb]
                tm = None
                for dc in range(8):
                    sc.op("T", lambda e, pb=pb, dc=dc, tt=tt, eb=eb: e.matmul(
                        pb[:, :], lnT[:, dc * S + tt * 128:dc * S + tt * 128 + 128],
                        self.wa[1][:, dc * D + eb * 512:dc * D + eb * 512 + 512],
                        start=(dc == 0), stop=False),
                        waits=([self.t_wv, self.t_bv, self.ps_free[4 + eb]] +
                               self.t_ones + tl(self.getf("big")))
                        if dc == 0 else [])
                tm = sc.op("T", lambda e, pb=pb, eb=eb: e.matmul(
                    pb[:, :], self.ones_col[:, :],
                    self.bv_bf[0:1, eb * 512:(eb + 1) * 512],
                    start=False, stop=True), sig=True)
                t_ev = sc.op("S", lambda e, pb=pb, tt=tt, eb=eb: e.activation(
                    v_sb[:, tt * D + eb * 512:tt * D + eb * 512 + 512], pb[:, :],
                    AF.Silu), waits=[tm], sig=True)
                self.ps_free[4 + eb] = t_ev
                evs.append(t_ev)
            t_v.append(evs)
        if b == 0:
            if "qkT" in self.taps:
                self.out_toks.append(sc.dma("Y", self.taps_d["qkT"].ap(), qk[:, :],
                                            waits=t_qk))
            if "qT" in self.taps:
                self.out_toks.append(sc.dma("Y", self.taps_d["qT"].ap(), qT[:, :],
                                            waits=[t_q]))
            if "kT" in self.taps:
                self.out_toks.append(sc.dma("Y", self.taps_d["kT"].ap(), kT[:, :],
                                            waits=[t_k]))
            if "v" in self.taps:
                self.out_toks.append(sc.dma("Y", self.taps_d["v"].ap(),
                                            v_sb[:, 0:16384],
                                            waits=[t for ev in t_v for t in ev]))

        slab = self.big[:, 16 * 1024:24 * 1024]
        slab_free = [None] * 16
        dst_x = self.d_xs0.ap()
        x_src = self.d_x.ap()
        for ib in range(4):
            for jt in range(4 * ib + 4):
                pb = self.ps[6 + (jt % 2)]
                tm = sc.op("T", lambda e, pb=pb, jt=jt, ib=ib: e.matmul(
                    pb[:, :], kT[:, jt * 128:(jt + 1) * 128],
                    qT[:, ib * 512:(ib + 1) * 512], start=True, stop=True),
                    waits=[t_q, t_k, self.ps_free[6 + (jt % 2)]], sig=True)
                t2 = self.scr_f32(1024 * (jt % 2), 512)
                off = ib * 512 - jt * 128 + 384
                tb1 = sc.op("V", lambda e, pb=pb, t2=t2, off=off: e.tensor_tensor(
                    t2, pb[:, :], self.megaT[:, off:off + 512], ALU.add),
                    waits=[tm, self.getf(f"t2_{jt % 2}")] +
                          (self.t_megaT if (ib == 0 and jt == 0) else []),
                    sig=True)
                self.ps_free[6 + (jt % 2)] = tb1
                tb2 = sc.op("S", lambda e, t2=t2: e.activation(t2, t2, AF.Erf),
                            waits=[tb1], sig=True)
                dsl = slab[:, jt * 512:(jt + 1) * 512]
                if jt >= 4 * ib:
                    sh = 128 * jt - 512 * ib
                    t4 = self.scr_f32(1024 * (jt % 2) + 512, 512)
                    tb3 = sc.op("V", lambda e, t2=t2, t4=t4: e.tensor_scalar(
                        t4, t2, 0.5, 0.5, ALU.mult, ALU.add), waits=[tb2])
                    tb4 = sc.op("V", lambda e, t4=t4, dsl=dsl, sh=sh:
                                e.tensor_tensor(dsl, t4,
                                                self.tmb[:, 384 - sh:
                                                         384 - sh + 512],
                                                ALU.mult),
                                waits=[self.t_tmb, slab_free[jt]], sig=True)
                else:
                    tb4 = sc.op("V", lambda e, t2=t2, dsl=dsl: e.tensor_scalar(
                        dsl, t2, 0.5, 0.5, ALU.mult, ALU.add),
                        waits=[tb2, slab_free[jt]], sig=True)
                self.setf(f"t2_{jt % 2}", tb4)
                slab_free[jt] = tb4
            if "attnT" in self.taps and b == 0 and ib == 3:
                self.out_toks.append(sc.dma("Y", self.taps_d["attnT"].ap(),
                                            slab[:, :], waits=list(slab_free)))
            for io in range(4):
                it = 4 * ib + io
                row0 = b * S + it * 128
                xi = it % 3
                xo = self.xold[xi]
                t_ldx = sc.dma("Y", xo[:, :], x_src[row0:row0 + 128, :],
                               waits=[self.getf(f"xold{xi}")])
                ii = it % 2
                toks = []
                for eb in range(2):
                    pb = self.ps[eb]
                    tm = None
                    for jt in range(it + 1):
                        tm = sc.op("T", lambda e, pb=pb, jt=jt, io=io, eb=eb,
                                   it=it: e.matmul(
                                       pb[:, :],
                                       slab[:, jt * 512 + io * 128:
                                            jt * 512 + io * 128 + 128],
                                       v_sb[:, jt * D + eb * 512:
                                            jt * D + eb * 512 + 512],
                                       start=(jt == 0), stop=(jt == it)),
                                   waits=([slab_free[j2] for j2 in range(it + 1)] +
                                          [t_v[j2][eb] for j2 in range(it + 1)] +
                                          [self.ps_free[eb]]) if jt == 0 else [],
                                   sig=(jt == it))
                    tev = sc.op("V", lambda e, pb=pb, xo=xo, eb=eb:
                                e.scalar_tensor_tensor(
                                    xo[:, eb * 512:(eb + 1) * 512], pb[:, :], 1.0,
                                    xo[:, eb * 512:(eb + 1) * 512], ALU.mult,
                                    ALU.add),
                                waits=[tm, t_ldx], sig=True)
                    self.ps_free[eb] = tev
                    toks.append(tev)
                fl = self.boundary2(dst_x, row0, xo[:, :], toks, ii,
                                    last=(self.upto == 1))
                self.setf(f"xold{xi}", fl)
                tb5 = fl[0]
        return tb5

    def ffn(self, call, b, h, x_src, dst_x, first_pass, lnT_toks, last=False):
        sc = self.sc
        lnT = self.lnT[b]
        tbase = h * HALF
        out_c = self.big_f32(0, 8 * 1024)
        sel_slots = [self.wa[0][:, 0:4096], self.wa[0][:, 4096:8192]]
        comp_slots = [self.wa[1][:, 0:4096], self.wa[1][:, 4096:8192]]
        selb_slots = [self.scr_f32(0, 512), self.scr_f32(512, 512)]
        stage = self.big[:, 16 * 1024:24 * 1024].bitcast(F32)
        lg = [self.scr_f32(1024, 512), self.scr_f32(1536, 512)]
        mtree = [self.scr_f32(2048, 448), self.scr_f32(2048 + 448, 448)]
        mask = [self.scr[:, 5888 + i * 512:5888 + (i + 1) * 512] for i in range(2)]
        sel = [self.scr_f32(3456, 512), self.scr_f32(3968, 512)]
        a1 = [self.scr_f32(4480, 384), self.scr_f32(4864, 384)]

        selw_bf_v = self.d_selw_bf.ap()
        compw_bf_v = self.d_compw_bf.ap()
        bnd_last = None
        for ob in range(NOB):
            sl = ob % 2
            if first_pass:
                t_ls = sc.dma("Y", stage[:, 0:4096],
                              self.d_selw.ap()
                              .rearrange("(dc p) k -> p dc k", p=128)
                              [:, :, ob * 512:(ob + 1) * 512],
                              waits=[self.getf("stage")])
                t_cs = sc.op("S", lambda e, sl=sl: e.copy(sel_slots[sl][:, :],
                                                          stage[:, 0:4096]),
                             waits=[t_ls, self.getf(f"sel{sl}")], sig=True)
                t_wb1 = sc.dma("Y", selw_bf_v[ob, :, :]
                               .rearrange("(dc p) k -> p dc k", p=128),
                               sel_slots[sl][:, :], waits=[t_cs])
                self.setf("stage", [t_cs])
                t_lc = sc.dma("Y", stage[:, 0:4096],
                              self.d_compw.ap()[ob, :, :]
                              .rearrange("(dc p) k -> p dc k", p=128),
                              waits=[self.getf("stage")])
                t_cc = sc.op("S", lambda e, sl=sl: e.copy(comp_slots[sl][:, :],
                                                          stage[:, 0:4096]),
                             waits=[t_lc, self.getf(f"comp{sl}")], sig=True)
                t_wb2 = sc.dma("Y", compw_bf_v[ob, :, :]
                               .rearrange("(dc p) k -> p dc k", p=128),
                               comp_slots[sl][:, :], waits=[t_cc])
                self.setf("stage", [t_cs, t_cc])
                t_sw, t_cw = t_cs, t_cc
                self.setf(f"selwb{ob}", t_wb1)
                self.setf(f"compwb{ob}", t_wb2)
            else:
                t_sw = sc.dma("Y", sel_slots[sl][:, :],
                              selw_bf_v[ob, :, :]
                              .rearrange("(dc p) k -> p dc k", p=128),
                              waits=[self.getf(f"sel{sl}"),
                                     self.getf(f"selwb{ob}")])
                t_cw = sc.dma("Y", comp_slots[sl][:, :],
                              compw_bf_v[ob, :, :]
                              .rearrange("(dc p) k -> p dc k", p=128),
                              waits=[self.getf(f"comp{sl}"),
                                     self.getf(f"compwb{ob}")])
            t_sb = sc.dma("Y", selb_slots[sl][:, :],
                          self.d_selbB.ap()[:, ob * 512:(ob + 1) * 512],
                          waits=[self.getf(f"selb{sl}")])
            t_cb = sc.dma("G", self.compb_s[sl][:, :],
                          self.d_compb.ap()[0:1, ob * 512:(ob + 1) * 512],
                          waits=[self.getf(f"compbs{sl}")])

            for tt in range(8):
                ttg = tbase // 128 + tt
                i2 = tt % 2
                pL = self.ps[i2]
                pC = self.ps[2 + i2]
                tmL = None
                for dc in range(8):
                    tmL = sc.op("T", lambda e, pL=pL, dc=dc, ttg=ttg, sl=sl:
                                e.matmul(
                                    pL[:, :],
                                    lnT[:, dc * S + ttg * 128:
                                        dc * S + ttg * 128 + 128],
                                    sel_slots[sl][:, dc * 512:(dc + 1) * 512],
                                    start=(dc == 0), stop=(dc == 7)),
                                waits=([t_sw, self.ps_free[i2]] + lnT_toks)
                                if dc == 0 else [], sig=(dc == 7))
                for dc in range(8):
                    sc.op("T", lambda e, pC=pC, dc=dc, ttg=ttg, sl=sl: e.matmul(
                        pC[:, :],
                        lnT[:, dc * S + ttg * 128:dc * S + ttg * 128 + 128],
                        comp_slots[sl][:, dc * 512:(dc + 1) * 512],
                        start=(dc == 0), stop=False),
                        waits=[t_cw, self.ps_free[2 + i2]] if dc == 0 else [])
                tmC = sc.op("T", lambda e, pC=pC, sl=sl: e.matmul(
                    pC[:, :], self.ones_col[:, :], self.compb_s[sl][:, :],
                    start=False, stop=True),
                    waits=tl(self.t_ones) + [t_cb], sig=True)
                lgi, mt, mki, seli, a1i = (lg[i2], mtree[i2], mask[i2], sel[i2],
                                           a1[i2])
                tv1 = sc.op("V", lambda e, lgi=lgi, pL=pL, sl=sl: e.tensor_tensor(
                    lgi, pL[:, :], selb_slots[sl][:, :], ALU.add),
                    waits=[tmL, t_sb, self.getf(f"lg{i2}")], sig=True)
                self.ps_free[i2] = tv1
                lgv = lgi.rearrange("p (g n) -> p g n", n=8)
                m1 = mt[:, 0:256].rearrange("p (g n) -> p g n", n=4)
                m2 = mt[:, 256:384].rearrange("p (g n) -> p g n", n=2)
                m3 = mt[:, 384:448]
                sc.op("V", lambda e, lgv=lgv, m1=m1: e.tensor_tensor(
                    m1, lgv[:, :, 0:4], lgv[:, :, 4:8], ALU.max))
                sc.op("V", lambda e, m1=m1, m2=m2: e.tensor_tensor(
                    m2, m1[:, :, 0:2], m1[:, :, 2:4], ALU.max))
                sc.op("V", lambda e, m2=m2, m3=m3: e.tensor_tensor(
                    m3, m2[:, :, 0:1].squeeze(2), m2[:, :, 1:2].squeeze(2),
                    ALU.max))
                tv5 = sc.op("V", lambda e, lgv=lgv, m3=m3, mki=mki: e.tensor_tensor(
                    mki.rearrange("p (g n) -> p g n", n=8), lgv,
                    m3.unsqueeze(2).broadcast_to([128, 64, 8]), ALU.is_ge),
                    sig=True)
                self.setf(f"lg{i2}", tv5)
                if ("ffdbg" in self.taps and call == 1 and b == 0 and h == 1
                        and ob == 0 and tt == 0):
                    self.out_toks.append(sc.dma("Y", self.taps_d["ffdbg"].ap(),
                                                lgi, waits=[tv1]))
                if ("ffdbgm" in self.taps and call == 1 and b == 0 and h == 1
                        and ob == 0 and tt == 0):
                    self.out_toks.append(sc.dma(
                        "Y", self.taps_d["ffdbgm"].ap()[:, 0:512], mki,
                        waits=[tv5]))
                mask_no = bass.AP(mki.tensor, mki.offset,
                                  [list(mki.ap)[0], [1, 8], [8, 64]])
                if ("psumc" in self.taps and call == 1 and b == 0 and h == 1
                        and ob == 0 and tt == 0):
                    tpd = sc.op("V", lambda e, pC=pC: e.tensor_copy(
                        self.scr_f32(1024, 512), pC[:, :]), waits=[tmC], sig=True)
                    self.out_toks.append(sc.dma("Y", self.taps_d["psumc"].ap(),
                                                self.scr_f32(1024, 512),
                                                waits=[tpd]))
                    if "slotc" in self.taps:
                        self.out_toks.append(sc.dma(
                            "Y", self.taps_d["slotc"].ap(),
                            comp_slots[sl][:, :], waits=[tpd]))
                tv7 = sc.op("V", lambda e, seli=seli, pC=pC, mask_no=mask_no:
                            e.tensor_tensor(
                                seli.rearrange("p (n g) -> p n g", n=8),
                                pC[:, :].rearrange("p (n g) -> p n g", n=8),
                                mask_no, ALU.mult),
                            waits=[tmC, tv5, self.getf(f"selscr{i2}")], sig=True)
                self.ps_free[2 + i2] = tv7
                sc.op("V", lambda e, a1i=a1i, seli=seli: e.tensor_tensor(
                    a1i[:, 0:256], seli[:, 0:256], seli[:, 256:512], ALU.add))
                sc.op("V", lambda e, a1i=a1i: e.tensor_tensor(
                    a1i[:, 256:384], a1i[:, 0:128], a1i[:, 128:256], ALU.add))
                oloc = out_c[:, tt * 1024 + ob * 64:tt * 1024 + ob * 64 + 64]
                tv9 = sc.op("V", lambda e, a1i=a1i, oloc=oloc: e.tensor_tensor(
                    oloc, a1i[:, 256:320], a1i[:, 320:384], ALU.add),
                    waits=[self.getf("outc")], sig=True)
                if ("ffdbgm" in self.taps and call == 1 and b == 0 and h == 1
                        and ob == 0 and tt == 0):
                    self.out_toks.append(sc.dma(
                        "Y", self.taps_d["ffdbgm"].ap()[:, 512:1024], seli,
                        waits=[tv7]))
                self.setf(f"selscr{i2}", tv9)
                if tt == 7:
                    self.setf(f"sel{sl}", tmL)
                    self.setf(f"comp{sl}", tmC)
                    self.setf(f"compbs{sl}", tmC)
                    self.setf(f"selb{sl}", tv1)
                    if ob == NOB - 1:
                        out_done = tv9
        self.setf("outc", None)
        for tt in range(8):
            ttg = tbase // 128 + tt
            row0 = b * S + ttg * 128
            xi = tt % 3
            xo = self.xold[xi]
            t_ldx = sc.dma("Y", xo[:, :], x_src[row0:row0 + 128, :],
                           waits=[self.getf(f"xold{xi}"), self.getf(f"xw{row0}")])
            ii = tt % 2
            toks = []
            tev = sc.op("V", lambda e, xo=xo, tt=tt: e.scalar_tensor_tensor(
                xo[:, :], out_c[:, tt * 1024:tt * 1024 + 1024], 1.0,
                xo[:, :], ALU.mult, ALU.add),
                waits=[t_ldx, out_done], sig=True)
            toks.append(tev)
            fl = self.boundary2(dst_x, row0, xo[:, :], toks, ii, last=last)
            self.setf(f"xold{xi}", fl)
            bnd_last = fl[0]
        self.setf("outc", bnd_last)
        if "outc" in self.taps and call == 1 and b == 0 and h == 0:
            self.out_toks.append(sc.dma("Y", self.taps_d["outc"].ap(), out_c,
                                        waits=[bnd_last]))
        return bnd_last

    def attn_b(self, b, x_src, dst_x, lnT_toks):
        sc = self.sc
        lnT = self.lnT[b]
        first = (b == 0)
        xin_f = self.big_f32(0, 2048)
        xin_b = self.big[:, 4096:6144]
        h_bf = self.big[:, 6144:8192]
        h32 = self.big_f32(4096, 2048)
        wout_sb = self.big[:, 12288:20480]
        stage = self.big_f32(10240, 2048)

        if first:
            self.t_wconv = {}
            for wi, (dsrc, slot, frees) in enumerate(
                    [(self.d_win, self.wa[2], [self.getf("wa2")]),
                     (self.d_wgx, self.wa[0],
                      [self.getf("sel0"), self.getf("sel1")]),
                     (self.d_wga, self.wa[1],
                      [self.getf("comp0"), self.getf("comp1")]),
                     (self.d_wout, wout_sb, [self.getf("outc")])]):
                toks = []
                for hh in range(4):
                    t_l = sc.dma("Y", stage[:, 0:2048],
                                 dsrc.ap().rearrange("(dc p) e -> p dc e", p=128)
                                 [:, :, hh * 256:(hh + 1) * 256],
                                 waits=[self.getf("stage2")])
                    t_c = sc.op("S", lambda e, slot=slot, hh=hh: e.copy(
                        slot[:, :].rearrange("p (dc e) -> p dc e", dc=8)
                        [:, :, hh * 256:(hh + 1) * 256], stage[:, 0:2048]),
                        waits=[t_l] + (frees if hh == 0 else []), sig=True)
                    self.setf("stage2", t_c)
                    toks.append(t_c)
                self.t_wconv[wi] = toks
        tw_in = self.t_wconv[0]
        tw_gx = self.t_wconv[1]
        tw_ga = self.t_wconv[2]
        tw_out = self.t_wconv[3]

        t_cz = sc.op("G", lambda e: e.memset(self.carry[:, :], 0.0),
                     waits=[self.getf("carry")], sig=True)

        gxs = self.scr_f32(0, 2048)
        gas = self.scr_f32(2048, 2048)
        aTs = self.scr_f32(4096, 2048)
        tb5 = None
        if self.ab_phase < 1:
            return tw_out[-1]
        for tblk in range(self.ab_tblks):
            t0 = tblk * 256
            t_xin = []
            for et in range(8):
                pb = self.ps[4 + et % 2]
                for dc in range(8):
                    sc.op("T", lambda e, pb=pb, dc=dc, et=et, t0=t0: e.matmul(
                        pb[:, 0:256], self.wa[2][:, dc * D + et * 128:
                                                 dc * D + et * 128 + 128],
                        lnT[:, dc * S + t0:dc * S + t0 + 256],
                        start=(dc == 0), stop=False),
                        waits=(tw_in + lnT_toks + [self.ps_free[4 + et % 2]])
                        if dc == 0 else [])
                tm = sc.op("T", lambda e, pb=pb, et=et: e.matmul(
                    pb[:, 0:256], self.bin_bf[0:1, et * 128:et * 128 + 128],
                    self.ones_row[0:1, 0:256], start=False, stop=True),
                    waits=[self.t_bin] + self.t_ones, sig=True)
                t_f = sc.op("S", lambda e, pb=pb, et=et: e.copy(
                    xin_f[:, et * 256:(et + 1) * 256], pb[:, 0:256]),
                    waits=[tm, self.getf(f"xinf{et}")], sig=True)
                t_b = sc.op("V", lambda e, et=et: e.tensor_copy(
                    xin_b[:, et * 256:(et + 1) * 256],
                    xin_f[:, et * 256:(et + 1) * 256]),
                    waits=[t_f, self.getf(f"xinb{et}")], sig=True)
                if ("gdbg2" in self.taps and b == 0 and tblk == 0):
                    td = sc.dma("Y", self.taps_d["gdbg2"].ap()
                                [:, et * 256:et * 256 + 256],
                                xin_f[:, et * 256:(et + 1) * 256], waits=[t_f])
                    self.out_toks.append(td)
                    self.setf(f"xinf{et}", [t_b, td])
                self.ps_free[4 + et % 2] = [t_f, t_b]
                t_xin.append([t_f, t_b])
            if self.ab_phase < 2:
                tb5 = t_xin[-1][0]
                for et in range(8):
                    self.setf(f"xinf{et}", t_xin[et][0])
                    self.setf(f"xinb{et}", t_xin[et][1])
                continue
            t_g = [[], []]
            for gi, (wsl, twsl, dst, bias) in enumerate(
                    [(self.wa[0], tw_gx, gxs, self.bgx),
                     (self.wa[1], tw_ga, gas, self.bga)]):
                for et in range(8):
                    pb = self.ps[gi * 2 + et % 2]
                    tm = None
                    for ec in range(8):
                        tm = sc.op("T", lambda e, pb=pb, ec=ec, et=et, wsl=wsl:
                                   e.matmul(
                                       pb[:, 0:256],
                                       wsl[:, ec * D + et * 128:
                                           ec * D + et * 128 + 128],
                                       xin_b[:, ec * 256:(ec + 1) * 256],
                                       start=(ec == 0), stop=(ec == 7)),
                            waits=(twsl + [self.ps_free[gi * 2 + et % 2],
                                           t_xin[ec][1]])
                            if ec == 0 else [t_xin[ec][1]],
                            sig=(ec == 7))
                    tev = sc.op("S", lambda e, pb=pb, dst=dst, et=et, bias=bias:
                                e.activation(dst[:, et * 256:(et + 1) * 256],
                                             pb[:, 0:256], AF.Sigmoid,
                                             bias=bias[:, et:et + 1]),
                                waits=[tm, self.getf(f"g{gi}_{et}")], sig=True)
                    self.ps_free[gi * 2 + et % 2] = tev
                    if ("gdbg2" in self.taps and b == 0 and tblk == 0
                            and gi == 0):
                        td = sc.dma("Y", self.taps_d["gdbg2"].ap()
                                    [:, 2048 + et * 256:2048 + et * 256 + 256],
                                    dst[:, et * 256:(et + 1) * 256], waits=[tev])
                        self.out_toks.append(td)
                        self.setf(f"g{gi}_{et}", [tev, td])
                    t_g[gi].append(tev)
            if self.ab_phase < 3:
                tb5 = t_g[1][-1]
                for et in range(8):
                    self.setf(f"xinf{et}", [t_xin[et][0], t_g[0][et], t_g[1][et]])
                    self.setf(f"xinb{et}", t_g[1][et])
                    self.setf(f"g0_{et}", t_g[0][et])
                    self.setf(f"g1_{et}", t_g[1][et])
                continue
            t_scan = []
            tcw = None
            for et in range(8):
                sl_a = aTs[:, et * 256:(et + 1) * 256]
                sl_gx = gxs[:, et * 256:(et + 1) * 256]
                sl_ga = gas[:, et * 256:(et + 1) * 256]
                sl_h32 = h32[:, et * 256:(et + 1) * 256]
                sl_hbf = h_bf[:, et * 256:(et + 1) * 256]
                sl_xf = xin_f[:, et * 256:(et + 1) * 256]
                ta = sc.op("S", lambda e, sl_a=sl_a, sl_ga=sl_ga, et=et:
                           e.activation(sl_a, sl_ga, AF.Exp,
                                        scale=self.sa8[:, et:et + 1]),
                           waits=[t_g[1][et], self.t_sa8,
                                  self.getf(f"a_{et}")], sig=True)
                tsq = sc.op("V", lambda e, sl_ga=sl_ga, sl_a=sl_a: e.tensor_tensor(
                    sl_ga, sl_a, sl_a, ALU.mult), waits=[ta], sig=True)
                tsc3 = sc.op("S", lambda e, sl_ga=sl_ga: e.activation(
                    sl_ga, sl_ga, AF.Sqrt, bias=self.onec[:, :], scale=-1.0),
                    waits=[tsq, self.t_onec], sig=True)
                txg = sc.op("V", lambda e, sl_gx=sl_gx, sl_xf=sl_xf:
                            e.tensor_tensor(sl_gx, sl_gx, sl_xf, ALU.mult),
                            waits=[t_g[0][et], t_xin[et][0]], sig=True)
                tu = sc.op("V", lambda e, sl_ga=sl_ga, sl_gx=sl_gx:
                           e.tensor_tensor(sl_ga, sl_ga, sl_gx, ALU.mult),
                           waits=[tsc3, txg], sig=True)
                tsc2 = sc.op("V", lambda e, sl_h32=sl_h32, sl_a=sl_a, sl_ga=sl_ga,
                             et=et: e.tensor_tensor_scan(
                                 sl_h32, sl_a, sl_ga, self.carry[:, et:et + 1],
                                 ALU.mult, ALU.add),
                             waits=[tu, t_cz, self.getf("h32"),
                                    self.getf(f"cw{et}")], sig=True)
                self.setf(f"a_{et}", tsc2)
                self.setf(f"g0_{et}", tsc2)
                self.setf(f"g1_{et}", tsc2)
                self.setf(f"xinf{et}", [tsc2, txg])
                self.setf(f"xinb{et}", tsc2)
                tcw = sc.op("V", lambda e, sl_h32=sl_h32, et=et: e.tensor_copy(
                    self.carry[:, et:et + 1], sl_h32[:, 255:256]),
                    waits=[tsc2], sig=True)
                self.setf(f"cw{et}", tcw)
                thb = sc.op("S", lambda e, sl_hbf=sl_hbf, sl_h32=sl_h32:
                            e.copy(sl_hbf, sl_h32),
                            waits=[tsc2, self.getf(f"hbf{et}")], sig=True)
                t_scan.append(thb)
            self.setf("carry", tcw)
            if ("gdbg" in self.taps and b == 0 and tblk == 0):
                for et in range(8):
                    tda = sc.dma(
                        "Y", self.taps_d["gdbg"].ap()[:, et * 256:et * 256 + 256],
                        aTs[:, et * 256:(et + 1) * 256], waits=[t_scan[et]])
                    tdb = sc.dma(
                        "Y", self.taps_d["gdbg"].ap()
                        [:, 2048 + et * 256:2048 + et * 256 + 256],
                        gas[:, et * 256:(et + 1) * 256], waits=[t_scan[et]])
                    tdc = sc.dma(
                        "Y", self.taps_d["gdbg"].ap()
                        [:, 4096 + et * 256:4096 + et * 256 + 256],
                        h32[:, et * 256:(et + 1) * 256], waits=[t_scan[et]])
                    tdd = sc.dma(
                        "Y", self.taps_d["gdbg"].ap()
                        [:, 6144 + et * 256:6144 + et * 256 + 256],
                        gxs[:, et * 256:(et + 1) * 256], waits=[t_scan[et]])
                    self.out_toks += [tda, tdb, tdc, tdd]
                    self.setf(f"a_{et}", [self.getf(f"a_{et}"), tda])
                    self.setf(f"g1_{et}", [self.getf(f"g1_{et}"), tdb])
                    self.setf(f"g0_{et}", [self.getf(f"g0_{et}"), tdd])
                    self.setf(f"hbf{et}", [self.getf(f"hbf{et}"), tdc])
                    self.setf("h32", [tl(self.getf("h32")) + [tdc]])
            if "h" in self.taps and b == 0:
                self.out_toks.append(sc.dma(
                    "Y", self.taps_d["h"].ap()[:, tblk * 2048:(tblk + 1) * 2048],
                    h_bf[:, :], waits=list(t_scan)))
            if self.ab_phase < 4:
                continue
            for tli in range(2):
                ttg = (t0 + tli * 128) // 128
                row0 = b * S + ttg * 128
                xi = ttg % 3
                xo = self.xold[xi]
                t_ldx = sc.dma("Y", xo[:, :], x_src[row0:row0 + 128, :],
                               waits=[self.getf(f"xold{xi}"),
                                      self.getf(f"xw{row0}")])
                ii = ttg % 2
                toks = []
                for eb in range(2):
                    pb = self.ps[4 + eb]
                    for et in range(8):
                        sc.op("T", lambda e, pb=pb, et=et, tli=tli, eb=eb:
                              e.matmul(pb[:, :],
                                       h_bf[:, et * 256 + tli * 128:
                                            et * 256 + tli * 128 + 128],
                                       wout_sb[:, et * D + eb * 512:
                                               et * D + eb * 512 + 512],
                                       start=(et == 0), stop=False),
                              waits=(t_scan + tw_out + [self.ps_free[4 + eb]])
                              if et == 0 else [])
                    tm = sc.op("T", lambda e, pb=pb, eb=eb: e.matmul(
                        pb[:, :], self.ones_col[:, :],
                        self.bout_bf[0:1, eb * 512:eb * 512 + 512],
                        start=False, stop=True), waits=[self.t_bout], sig=True)
                    tev = sc.op("V", lambda e, pb=pb, xo=xo, eb=eb:
                                e.scalar_tensor_tensor(
                                    xo[:, eb * 512:(eb + 1) * 512], pb[:, :], 1.0,
                                    xo[:, eb * 512:(eb + 1) * 512], ALU.mult,
                                    ALU.add),
                                waits=[tm, t_ldx], sig=True)
                    self.ps_free[4 + eb] = tev
                    toks.append(tev)
                fl = self.boundary2(dst_x, row0, xo[:, :], toks, ii,
                                    last=(self.upto == 3))
                self.setf(f"xold{xi}", fl)
                tb5 = fl[0]
            for et in range(8):
                self.setf(f"hbf{et}", tb5)
            self.setf("h32", tb5)
        return tb5

    def build(self):
        sc = self.sc
        self.p0()
        self.ln_w_toks = []
        self.ln1(0)
        lnw0 = list(self.ln_w_toks)
        self.ln_w_toks = []
        tA = self.transposes(0, 0, lnw0)
        self.ln1(1)
        lnw1 = list(self.ln_w_toks)
        self.ln_w_toks = []
        tB = self.transposes(1, 1, lnw1, extra=self.t_rel_done)
        self.setf("lnT0", tA[-1])
        self.setf("lnT1", tB[-1])
        if "ln1T" in self.taps:
            self.out_toks.append(sc.dma("Y", self.taps_d["ln1T"].ap(),
                                        self.lnT[0][:, :], waits=tA))
        f0 = self.attn_a(0, tA)
        lnw0 = list(self.ln_w_toks)
        self.ln_w_toks = []
        f1 = self.attn_a(1, tB)
        lnw1 = list(self.ln_w_toks)
        self.ln_w_toks = []
        if "xa" in self.taps:
            self.out_toks.append(sc.dma("Y", self.taps_d["xa"].ap(),
                                        self.d_xs0.ap(), waits=[f1]))
        if self.upto == 1:
            self.finish(None)
            return
        self.fence(["sel0", "sel1", "comp0", "comp1", "selb0", "selb1",
                    "stage", "lg0", "lg1", "selscr0", "selscr1", "outc",
                    "wa2"], f1)
        tA = self.transposes(0, 0, lnw0, extra=f0)
        tB2 = self.transposes(1, 1, lnw1, extra=f1)
        self.setf("lnT0", tA[-1])
        self.setf("lnT1", tB2[-1])
        if "ln2T" in self.taps:
            self.out_toks.append(sc.dma("Y", self.taps_d["ln2T"].ap(),
                                        self.lnT[0][:, :], waits=tA))
        for b in range(2):
            tk2 = tA if b == 0 else tB2
            for h in range(2):
                fb = self.ffn(1, b, h, self.d_xs0.ap(), self.d_xs1.ap(),
                              first_pass=(b == 0 and h == 0), lnT_toks=tk2,
                              last=(self.upto == 2))
            if b == 0:
                lnw0 = list(self.ln_w_toks)
                self.ln_w_toks = []
                ff0 = fb
            else:
                lnw1 = list(self.ln_w_toks)
                self.ln_w_toks = []
                ff1 = fb
        if "xf" in self.taps:
            self.out_toks.append(sc.dma("Y", self.taps_d["xf"].ap(),
                                        self.d_xs1.ap(), waits=[ff1]))
        if self.upto == 2:
            self.finish(None)
            return
        self.fence(["stage2", "h32"] +
                   [f"xinf{e}" for e in range(8)] +
                   [f"xinb{e}" for e in range(8)] +
                   [f"g{g}_{e}" for g in range(2) for e in range(8)] +
                   [f"hbf{e}" for e in range(8)] +
                   [f"a_{e}" for e in range(8)] +
                   [f"cw{e}" for e in range(8)], ff1)
        tA = self.transposes(0, 0, lnw0, extra=ff0)
        tB2 = self.transposes(1, 1, lnw1, extra=ff1)
        self.setf("lnT0", tA[-1])
        self.setf("lnT1", tB2[-1])
        fb0 = self.attn_b(0, self.d_xs1.ap(), self.d_xs0.ap(), tA)
        lnw0 = list(self.ln_w_toks)
        self.ln_w_toks = []
        if self.ab_batches > 1:
            fb1 = self.attn_b(1, self.d_xs1.ap(), self.d_xs0.ap(), tB2)
        else:
            fb1 = fb0
        lnw1 = list(self.ln_w_toks)
        self.ln_w_toks = []
        if "xb" in self.taps:
            self.out_toks.append(sc.dma("Y", self.taps_d["xb"].ap(),
                                        self.d_xs0.ap(), waits=[fb1]))
        if self.upto == 3:
            self.finish(None)
            return
        self.fence(["sel0", "sel1", "comp0", "comp1", "selb0", "selb1",
                    "lg0", "lg1", "selscr0", "selscr1", "outc"], fb1)
        tA = self.transposes(0, 0, lnw0, extra=fb0)
        tB2 = self.transposes(1, 1, lnw1, extra=fb1)
        for b in range(2):
            tk2 = tA if b == 0 else tB2
            for h in range(2):
                self.ffn(2, b, h, self.d_xs0.ap(), self.d_y.ap(),
                         first_pass=False, lnT_toks=tk2, last=True)
        self.finish(None)

    def finish(self, _unused=None):
        sc = self.sc
        if self.upto < 4:
            src = {1: self.d_xs0, 2: self.d_xs1, 3: self.d_xs0}[self.upto]
            allw = list(self.out_toks)
            for tt in range(32):
                xi = tt % 3
                xo = self.xold[xi]
                t_l = sc.dma("Y", xo[:, :], src.ap()[tt * 128:(tt + 1) * 128, :],
                             waits=allw + [self.getf(f"xold{xi}")])
                t_w = sc.dma("Y", self.d_y.ap()[tt * 128:(tt + 1) * 128, :],
                             xo[:, :], waits=[t_l])
                self.setf(f"xold{xi}", t_w)
                self.out_toks.append(t_w)
        ws = []
        for w in self.out_toks:
            ws += tl(w)
        sc.ops["Y"].append((lambda e: e.nop(), ws, None))
        sc.emit_all()


def host_inputs(inp, core):
    x = inp["L0_attn_a_x"]
    gamma = np.asarray(inp["L0_attn_a_gamma"])
    beta = np.asarray(inp["L0_attn_a_beta"])
    comp_w = np.asarray(inp["L0_ffn_comp_w"])
    m = {}
    m["x"] = np.ascontiguousarray(np.asarray(x)[2 * core:2 * core + 2]
                                  .reshape(B * S, D))
    m["wqk"] = np.ascontiguousarray(inp["L0_attn_a_Wqk"])
    m["bqk"] = np.ascontiguousarray(np.asarray(inp["L0_attn_a_bqk"])
                                    .reshape(EQK, 1))
    m["g0"] = np.ascontiguousarray(gamma[0].reshape(EQK, 1))
    m["b0"] = np.ascontiguousarray(beta[0].reshape(EQK, 1))
    m["g1"] = np.ascontiguousarray(gamma[1].reshape(EQK, 1))
    m["b1"] = np.ascontiguousarray(beta[1].reshape(EQK, 1))
    m["wv"] = np.ascontiguousarray(inp["L0_attn_a_Wv"])
    m["bv"] = np.ascontiguousarray(np.asarray(inp["L0_attn_a_bv"]).reshape(1, D))
    m["rel"] = np.ascontiguousarray(inp["L0_attn_a_rel_emb"])
    m["oht"] = _oht()
    m["tmb"] = _tmb()
    m["selw"] = np.ascontiguousarray(np.asarray(inp["L0_ffn_sel_w"]).T)
    m["selbB"] = np.ascontiguousarray(
        np.broadcast_to(np.asarray(inp["L0_ffn_sel_b"])[None, :], (128, KSEL)))
    m["compw"] = np.ascontiguousarray(
        comp_w.transpose(2, 0, 1).reshape(D, 8, NOB, 64)
        .transpose(2, 0, 1, 3).reshape(NOB, D, 512))
    m["compb"] = np.ascontiguousarray(
        np.asarray(inp["L0_ffn_comp_b"]).reshape(8, NOB, 64).transpose(1, 0, 2)
        .reshape(1, KSEL))
    m["win"] = np.ascontiguousarray(inp["L0_attn_b_W_in"])
    m["bin"] = np.ascontiguousarray(np.asarray(inp["L0_attn_b_b_in"])
                                    .reshape(1, D))
    m["wgx"] = np.ascontiguousarray(inp["L0_attn_b_W_gx"])
    m["bgx"] = np.ascontiguousarray(np.asarray(inp["L0_attn_b_b_gx"])
                                    .reshape(8, 128).T)
    m["wga"] = np.ascontiguousarray(inp["L0_attn_b_W_ga"])
    m["bga"] = np.ascontiguousarray(np.asarray(inp["L0_attn_b_b_ga"])
                                    .reshape(8, 128).T)
    m["apar"] = np.ascontiguousarray(np.asarray(inp["L0_attn_b_a_param"])
                                     .reshape(8, 128).T)
    m["wout"] = np.ascontiguousarray(inp["L0_attn_b_W_out"])
    m["bout"] = np.ascontiguousarray(np.asarray(inp["L0_attn_b_b_out"])
                                     .reshape(1, D))
    return m


def _oht():
    import math
    NB, MD = 32, 128
    n = np.arange(S)
    me = NB // 2
    is_small = n < me
    nf = np.maximum(n, 1).astype(np.float32)
    vl = me + (np.log(nf / me) / math.log(MD / me) * (NB - me)).astype(np.int32)
    vl = np.minimum(vl, NB - 1)
    buckets = np.where(is_small, n, vl)
    oht = np.zeros((32, S), np.float32)
    oht[buckets, np.arange(S)] = 1.0
    return oht


def _tmb():
    p = np.arange(128)[:, None]
    c = np.arange(896)[None, :]
    return (c >= p + 384).astype(np.float32)


def kernel(**inputs):
    mkk = MK(upto=4, taps=())
    mkk.build()
    base = host_inputs(inputs, 0)
    in_maps = []
    x_full = np.ascontiguousarray(np.asarray(inputs["L0_attn_a_x"],
                                             dtype=np.float32))
    for core in range(8):
        mm = dict(base)
        mm["x"] = np.ascontiguousarray(
            x_full[2 * core:2 * core + 2].reshape(B * S, D))
        in_maps.append(mm)
    res = run_bass_kernel_spmd(mkk.nc, in_maps, core_ids=list(range(8)))
    out = np.concatenate([r["y"].reshape(2, S, D) for r in res.results], axis=0)
    return out
